# version 01
# baseline (speedup 1.0000x reference)
import contextlib
import os
import numpy as np

import concourse.bass as bass
import concourse.mybir as mybir
from concourse.bass_utils import run_bass_kernel_spmd

F32 = mybir.dt.float32
BF16 = mybir.dt.bfloat16
U32 = mybir.dt.uint32
A = mybir.AluOpType
AF = mybir.ActivationFunctionType
AX = mybir.AxisListType

S = 2048
D = 1024
H = 4096
E = 8
QK = 16
NTS = S // 128
NDC = D // 128
NHC = H // 128
NQ = 4
HCQ = NHC // NQ
EPS = 1e-5
MASK_ADD = -30000.0


class Prog:
    ENGINES = ("tensor", "vector", "scalar", "sync", "gpsimd")

    def __init__(self):
        self.ops = {e: [] for e in self.ENGINES}
        self.ctr = {}

    def bump(self, sem, n=1):
        self.ctr[sem] = self.ctr.get(sem, 0) + n
        return self.ctr[sem]

    def on(self, eng, fn, wait=(), inc=None, inc_n=1):
        ev = (inc, self.bump(inc, inc_n)) if inc is not None else None
        self.ops[eng].append((fn, [w for w in wait if w], inc, inc_n))
        return ev

    def emit(self, name, eng, sems):
        last = {}
        for fn, waits, inc, inc_n in self.ops[name]:
            for sm, val in waits:
                if val > 0 and val > last.get(sm, 0):
                    eng.wait_ge(sems[sm], val)
                    last[sm] = val
            ins = fn(eng)
            if inc is not None:
                if ins is None:
                    ins = eng.drain()
                ins.then_inc(sems[inc], inc_n)


def build_nc(dbg=(), phases=99):
    nc = bass.Bass("TRN2", target_bir_lowering=False,
                   dynamic_dma_scratch_size=16384)
    ctx = contextlib.ExitStack()

    hx = nc.dram_tensor("x_in", [S, D], F32, kind="ExternalInput").ap()
    hwq = nc.dram_tensor("w_q", [QK, D], F32, kind="ExternalInput").ap()
    hbq = nc.dram_tensor("b_q", [QK, 1], F32, kind="ExternalInput").ap()
    hwk = nc.dram_tensor("w_k", [QK, D], F32, kind="ExternalInput").ap()
    hbk = nc.dram_tensor("b_k", [QK, 1], F32, kind="ExternalInput").ap()
    hwg = nc.dram_tensor("wg", [D, E], F32, kind="ExternalInput").ap()
    hbg = nc.dram_tensor("bg_r", [1, E], F32, kind="ExternalInput").ap()
    hw1 = nc.dram_tensor("w1", [E, D, H], F32, kind="ExternalInput").ap()
    hb1 = nc.dram_tensor("b1", [E, H], F32, kind="ExternalInput").ap()
    hw2 = nc.dram_tensor("w2", [E, H, D], F32, kind="ExternalInput").ap()
    hb2 = nc.dram_tensor("b2", [E, D], F32, kind="ExternalInput").ap()
    hc = nc.dram_tensor("cvec", [1, D], F32, kind="ExternalInput").ap()
    hwmod = nc.dram_tensor("wmod", [D, 6 * D], F32, kind="ExternalInput").ap()
    hout = nc.dram_tensor("out", [S, D], F32, kind="ExternalOutput").ap()

    sbuf = lambda name, shape, dt: ctx.enter_context(nc.sbuf_tensor(name, shape, dt))
    x_sb = sbuf("x_sb", [128, NTS * D], BF16)
    harena = sbuf("harena", [128, NTS * D], BF16)
    out_sb = sbuf("outsb", [128, NTS * D], BF16)
    w1b = [sbuf(f"w1b{i}", [128, NDC * D], BF16) for i in range(2)]
    w2b = [sbuf(f"w2b{i}", [128, HCQ * D], BF16) for i in range(2)]
    scb = [sbuf(f"scb{i}", [128, S], BF16) for i in range(2)]
    ascr = sbuf("ascr", [128, 6144], BF16)
    h1T = sbuf("h1T", [128, HCQ * 512], BF16)
    ca_b = sbuf("ca_b", [128, D], BF16)
    cb_b = sbuf("cb_b", [128, D], BF16)
    sb_b = sbuf("sb_b", [128, D], BF16)
    ga_b = sbuf("ga_b", [128, D], BF16)
    gb_b = sbuf("gb_b", [128, D], BF16)
    wqT = sbuf("wqT", [128, NDC * QK], BF16)
    wkT = sbuf("wkT", [128, NDC * QK], BF16)
    wg_sb = sbuf("wg_sb", [128, NDC * E], BF16)
    b1c = sbuf("b1c", [128, NHC * E], F32)
    sel = sbuf("sel", [128, NTS * E], F32)
    g_tok = sbuf("g_tok", [128, NTS * E], F32)
    scol = sbuf("scol", [128, 2], F32)
    bgrow = sbuf("bgrow", [1, E], BF16)
    bg_bc = sbuf("bg_bc", [128, E], F32)
    idx8 = sbuf("idx8", [128, 8], U32)
    idxf = sbuf("idxf", [128, NTS], F32)
    max8 = sbuf("max8", [128, 8], F32)
    iotaE = sbuf("iotaE", [128, E], F32)
    identb = sbuf("identb", [128, 128], BF16)
    cmask = sbuf("cmask", [128, 128], BF16)
    ones_r = sbuf("ones_r", [1, 128], BF16)
    sc_cols = sbuf("sc_cols", [128, NDC], BF16)
    sum_c = sbuf("sum_c", [128, NTS], F32)
    ssq_c = sbuf("ssq_c", [128, NTS], F32)
    m_c = sbuf("m_c", [128, NTS], F32)
    rs_c = sbuf("rs_c", [128, NTS], F32)
    nm_c = sbuf("nm_c", [128, NTS], F32)
    tm_c = sbuf("tm_c", [128, NTS], F32)
    mx_c = sbuf("mx_c", [128, 8], F32)

    x_v = x_sb[:].rearrange("q (ts d) -> q ts d", ts=NTS)
    qT = out_sb[0:QK, 0:S]
    kT = out_sb[0:QK, S:2 * S]
    c_row = scb[0][0:1, 0:D]
    scrow = scb[1][0:1, 0:D]
    b2rows = scb[0][0:8, 0:D]
    b2gb = scb[1][0:8, 0:D]
    selbf = scb[0][:, D:D + 8]
    mod_row = ascr[0:1, :]
    t_abs = kT[0:1, 0:D]
    t_sgn = qT[0:1, 0:D]
    b1rows = ascr[32:40, 0:D]
    wq_rows = ascr[64:80, 2 * D:3 * D]
    wk_rows = ascr[64:80, 3 * D:4 * D]
    strip = ascr[:, 0:4096]
    probs = ascr[:, 0:2048]
    probsT = ascr[:, 2048:4096]
    selT = ascr[0:8, 0:2048]
    scrA = ascr[:, 4096:5120]
    scrB = ascr[:, 5120:6144]
    scrC = ascr[:, 0:1024]
    sa_b = ascr[:, 1024:2048]

    psum = lambda name, shape, dt: ctx.enter_context(nc.psum_tensor(name, shape, dt))
    pA = [psum("pA0", [128, 512], F32), psum("pA1", [128, 512], F32)]
    pB = [psum("pB0", [128, 512], F32), psum("pB1", [128, 512], F32)]
    pT = [psum("pT0", [128, 1024], BF16), psum("pT1", [128, 1024], BF16)]
    pC = [psum("pC0", [128, 512], F32), psum("pC1", [128, 512], F32)]

    sems = {n: ctx.enter_context(nc.semaphore(n))
            for n in ["SM", "WM", "X", "W", "O", "T", "V", "S", "G"]}
    p = Prog()

    def t_mm(out, lhsT, rhs, start, stop, wait=(), inc=None):
        return p.on("tensor",
                    lambda t, o=out, l=lhsT, r=rhs, s1=start, s2=stop:
                    t.matmul(o, l, r, start=s1, stop=s2), wait=wait, inc=inc)

    def t_tr(out, in_, idn, wait=(), inc=None):
        return p.on("tensor", lambda t, o=out, i=in_, d=idn: t.transpose(o, i, d),
                    wait=wait, inc=inc)

    def v(fn, wait=(), inc=None):
        return p.on("vector", fn, wait=wait, inc=inc)

    def vr(fn, wait=(), inc=None):
        def g(eng, f=fn):
            f(eng)
            return eng.engine_nop()
        return p.on("vector", g, wait=wait, inc=inc)

    def s(fn, wait=(), inc=None):
        return p.on("scalar", fn, wait=wait, inc=inc)

    def v_nop():
        p.on("vector", lambda e: e.engine_nop())

    def s_table6():
        def f(e):
            return e.add_instruction(mybir.InstLoadActFuncSet(
                name=e.bass.get_next_instruction_name(), act_func_set_id=6,
                ins=[], outs=[]))
        return p.on("scalar", f)

    def gd(out, in_, wait=(), sem="W", slow=False):
        return p.on("gpsimd",
                    lambda g, o=out, i=in_, sl=slow:
                    g.dma_start(out=o, in_=i, allow_slow_non_contiguous=sl),
                    wait=wait, inc=sem, inc_n=16)

    def g_setup(g):
        g.memset(identb[:], 0.0)
        g.affine_select(out=identb[:], in_=identb[:], compare_op=A.not_equal,
                        fill=1.0, base=0, pattern=[[-1, 128]], channel_multiplier=1)
        g.memset(cmask[:], 0.0)
        g.affine_select(out=cmask[:], in_=cmask[:], compare_op=A.is_ge,
                        fill=MASK_ADD, base=0, pattern=[[-1, 128]], channel_multiplier=1)
        g.iota(iotaE[:], pattern=[[1, E]], base=0, channel_multiplier=0,
               allow_small_or_imprecise_dtypes=True)
        return g.memset(ones_r[:], 1.0)
    evG = p.on("gpsimd", g_setup, inc="G")

    gd(c_row[:], hc[:, :], sem="SM")
    gd(wq_rows, hwq[:, :], sem="SM")
    gd(wk_rows, hwk[:, :], sem="SM")
    gd(scol[0:QK, 0:1], hbq[:, :], sem="SM", slow=True)
    gd(scol[0:QK, 1:2], hbk[:, :], sem="SM", slow=True)
    gd(bgrow[:], hbg[:, :], sem="SM")
    gd(wg_sb[:].rearrange("q (c e) -> q c e", c=NDC),
       hwg.rearrange("(c q) e -> q c e", q=128), sem="SM")
    evSM = ("SM", p.ctr["SM"])

    msub = int(os.environ.get("KMSUB", "99"))
    ev_final = [None] * NTS
    if msub < 1:
        return _finish(nc, p, sems, ctx, dbg, locals())
    s(lambda e: e.preload_act_table(AF.Silu))
    evSilu = s(lambda e: e.activation(scrow[:], c_row[:], AF.Silu),
               wait=[evSM], inc="S")
    ev_sc = []
    for dc in range(NDC):
        wv = [evSilu, evG]
        if dc >= 2:
            wv.append(ev_sc[dc - 2][1])
        evt = t_tr(pT[dc % 2][:, 0:1], scrow[0:1, dc * 128:(dc + 1) * 128],
                   identb[0:1, 0:1], wait=wv, inc="T")
        eve = v(lambda e, d=dc, b=dc % 2: e.tensor_copy(sc_cols[:, d:d + 1],
                                                        pT[b][:, 0:1]),
                wait=[evt], inc="V")
        ev_sc.append((evt, eve))
    ev_sc_done = ev_sc[-1][1]

    if msub < 2:
        return _finish(nc, p, sems, ctx, dbg, locals())
    NMB = 12
    wmod_r = hwmod.rearrange("(c q) m -> q c m", q=128)
    w1b_v = [w1b[i][:, 0:NDC * 512].rearrange("q (c m) -> q c m", c=NDC)
             for i in range(2)]
    ev_modmm = [None] * NMB
    ev_modev = [None] * NMB
    for nb in range(NMB):
        par = nb % 2
        wv = [evG]
        if nb >= 2:
            wv.append(ev_modmm[nb - 2])
        evd = gd(w1b_v[par], wmod_r[:, :, nb * 512:(nb + 1) * 512], wait=wv, sem="WM")
        mmw = [evd, ev_sc_done]
        if nb >= 2:
            mmw.append(ev_modev[nb - 2])
        evm = None
        for dc in range(NDC):
            evm = t_mm(pA[par][0:1, :], sc_cols[:, dc:dc + 1],
                       w1b[par][:, dc * 512:(dc + 1) * 512],
                       start=(dc == 0), stop=(dc == NDC - 1),
                       wait=mmw if dc == 0 else (),
                       inc="T" if dc == NDC - 1 else None)
        ev_modmm[nb] = evm
        ev_modev[nb] = v(lambda e, n=nb, b=par:
                         e.tensor_copy(mod_row[0:1, n * 512:(n + 1) * 512],
                                       pA[b][0:1, :]),
                         wait=[evm], inc="V")

    if msub < 3:
        return _finish(nc, p, sems, ctx, dbg, locals())
    x_r = hx.rearrange("(ts q) d -> q ts d", q=128)
    ev_x = [gd(x_v[:, i * 4:(i + 1) * 4, :], x_r[:, i * 4:(i + 1) * 4, :], sem="X")
            for i in range(4)]

    if msub < 4:
        return _finish(nc, p, sems, ctx, dbg, locals())
    def clamp(col0, wv):
        gsl = mod_row[0:1, col0:col0 + D]
        e1 = s(lambda e, g=gsl: e.activation(t_abs, g, AF.Abs), wait=wv, inc="S")
        v(lambda e: e.tensor_scalar(t_abs, t_abs, 0.25, None, A.max),
          wait=[e1], inc="V")
        v(lambda e, g=gsl: e.tensor_scalar(t_sgn, g, 0.0, None, A.is_ge),
          wait=wv, inc="V")
        v_nop()
        v(lambda e: e.tensor_scalar(t_sgn, t_sgn, 2.0, -1.0, A.mult, A.add), inc="V")
        v_nop()
        return v(lambda e, g=gsl: e.tensor_tensor(g, t_sgn, t_abs, A.mult), inc="V")
    ev_ga = clamp(2 * D, [ev_modev[4], ev_modev[5]])
    ev_gb = clamp(5 * D, [ev_modev[10], ev_modev[11], ev_ga])

    if msub < 5:
        return _finish(nc, p, sems, ctx, dbg, locals())
    ev_pc_free = [None, None]
    def bcast(vec, dst, extra=()):
        eve = None
        for j in range(2):
            bank = j % 2
            wv = [ev_modev[vec * 2 + j]] + list(extra)
            if ev_pc_free[bank]:
                wv.append(ev_pc_free[bank])
            evm = t_mm(pC[bank][:], ones_r[:],
                       mod_row[0:1, vec * D + j * 512:vec * D + (j + 1) * 512],
                       start=True, stop=True, wait=wv, inc="T")
            eve = v(lambda e, d=dst, jj=j, b=bank:
                    e.tensor_copy(d[:, jj * 512:(jj + 1) * 512], pC[b][:]),
                    wait=[evm], inc="V")
            ev_pc_free[bank] = eve
        return eve
    ev_ca = bcast(1, ca_b)
    ev_sa = bcast(0, sa_b)
    ev_gab = bcast(2, ga_b, extra=[ev_ga])
    ev_sb2 = bcast(3, sb_b)
    ev_cb = bcast(4, cb_b)
    ev_gbb = bcast(5, gb_b, extra=[ev_gb])

    if msub < 6:
        return _finish(nc, p, sems, ctx, dbg, locals())
    wv = [evSM, ev_pc_free[0]]
    evm = t_mm(pC[0][:, 0:E], ones_r[:], bgrow[:], start=True, stop=True,
               wait=wv, inc="T")
    ev_bgbc = v(lambda e: e.tensor_copy(bg_bc[:], pC[0][:, 0:E]),
                wait=[evm], inc="V")
    ev_pc_free[0] = ev_bgbc

    ev_b2gb = None
    for j in range(2):
        bank = j % 2
        wv = [ev_gb, evSM]
        if ev_pc_free[bank]:
            wv.append(ev_pc_free[bank])
        evm = t_mm(pC[bank][0:8, :], ones_r[0:1, 0:8],
                   mod_row[0:1, 5 * D + j * 512:5 * D + (j + 1) * 512],
                   start=True, stop=True, wait=wv, inc="T")
        ev_b2gb = v(lambda e, jj=j, b=bank:
                    e.tensor_tensor(b2gb[0:8, jj * 512:(jj + 1) * 512],
                                    b2rows[0:8, jj * 512:(jj + 1) * 512],
                                    pC[b][0:8, :], A.mult),
                    wait=[evm], inc="V")
        ev_pc_free[bank] = ev_b2gb

    if msub < 7:
        return _finish(nc, p, sems, ctx, dbg, locals())
    ev_ptfree = [ev_sc[-2][1], ev_sc[-1][1]]
    ev_wqkT = None
    for i, (rows, dstT) in enumerate(((wq_rows, wqT), (wk_rows, wkT))):
        for dc in range(NDC):
            bank = (i * NDC + dc) % 2
            wv = [evSM, evG, ev_ptfree[bank]]
            evt = t_tr(pT[bank][:, 0:QK], rows[0:QK, dc * 128:(dc + 1) * 128],
                       identb[64:64 + QK, 64:64 + QK], wait=wv, inc="T")
            ev_wqkT = v(lambda e, d=dstT, c=dc, b=bank:
                        e.tensor_copy(d[:, c * QK:(c + 1) * QK], pT[b][:, 0:QK]),
                        wait=[evt], inc="V")
            ev_ptfree[bank] = ev_wqkT
    ev_bq4 = v(lambda e: e.tensor_scalar(scol[0:QK, 0:1], scol[0:QK, 0:1],
                                         0.25, None, A.mult), wait=[evSM], inc="V")

    if msub < 8:
        return _finish(nc, p, sems, ctx, dbg, locals())
    ev_b1t = []
    for q in range(NQ):
        wv = [evSM]
        if q >= 1:
            wv.append(ev_b1t[-1][0])
        evd = gd(b1rows, hb1[:, q * D:(q + 1) * D], wait=wv, sem="SM")
        for hcl in range(HCQ):
            hc = q * HCQ + hcl
            bank = hc % 2
            wv2 = [evd, ev_ptfree[bank]]
            evt = t_tr(pT[bank][:, 0:8], b1rows[0:8, hcl * 128:(hcl + 1) * 128],
                       identb[32:40, 32:40], wait=wv2, inc="T")
            eve = v(lambda e, h=hc, b=bank:
                    e.tensor_copy(b1c[:, h * E:(h + 1) * E], pT[b][:, 0:8]),
                    wait=[evt], inc="V")
            ev_b1t.append((evt, eve))
            ev_ptfree[bank] = eve
    ev_b1_done = ev_b1t[-1][1]

    if phases < 2:
        return _finish(nc, p, sems, ctx, dbg, locals())
    s_table6()
    ev_mod_dead = [ev_gbb, ("T", ev_b1t[-1][0][1])]
    ev_stat = []
    for ts in range(NTS):
        wx = [ev_x[ts // 4]]
        e1 = vr(lambda e, t=ts: e.reduce_sum(sum_c[:, t:t + 1], x_v[:, t, :],
                                             axis=AX.X), wait=wx, inc="V")
        e2 = s(lambda e, t=ts: e.activation(scrB, x_v[:, t, :], AF.Square,
                                            accum_out=ssq_c[:, t:t + 1]),
               wait=wx + ev_mod_dead, inc="S")
        ev_stat.append((e1, e2))

    def ln_cols(ev_sum, ev_ssq):
        v(lambda e: e.tensor_scalar(m_c[:], sum_c[:], 1.0 / D, None, A.mult),
          wait=[ev_sum], inc="V")
        v(lambda e: e.tensor_scalar(rs_c[:], ssq_c[:], 1.0 / D, None, A.mult),
          wait=[ev_ssq], inc="V")
        v_nop()
        v(lambda e: e.tensor_tensor(nm_c[:], m_c[:], m_c[:], A.mult), inc="V")
        v_nop()
        v(lambda e: e.tensor_sub(rs_c[:], rs_c[:], nm_c[:]), inc="V")
        v_nop()
        evv = v(lambda e: e.tensor_scalar(rs_c[:], rs_c[:], EPS, None, A.add),
                inc="V")
        s(lambda e: e.activation(tm_c[:], rs_c[:], AF.Ln), wait=[evv], inc="S")
        evs = s(lambda e: e.activation(rs_c[:], tm_c[:], AF.Exp, scale=-0.5),
                inc="S")
        v(lambda e: e.tensor_tensor(nm_c[:], m_c[:], rs_c[:], A.mult),
          wait=[evs], inc="V")
        v_nop()
        return v(lambda e: e.tensor_scalar(nm_c[:], nm_c[:], -1.0, None, A.mult),
                 inc="V")
    ev_ln1 = ln_cols(ev_stat[-1][0], ev_stat[-1][1])

    ev_hv = [None] * NTS
    ev_slot = [None, None]
    for ts in range(NTS):
        slot = [scrA, scrB][ts % 2]
        wv = [ev_ln1]
        if ev_slot[ts % 2]:
            wv.append(ev_slot[ts % 2])
        e1 = s(lambda e, t=ts, sl=slot:
               e.activation(sl, x_v[:, t, :], AF.Identity,
                            scale=rs_c[:, t:t + 1], bias=nm_c[:, t:t + 1]),
               wait=wv + ev_mod_dead, inc="S")
        e2 = v(lambda e, sl=slot: e.scalar_tensor_tensor(scrC, ca_b[:], 1.0, sl,
                                                         A.add, A.mult),
               wait=[e1, ev_ca] + ev_mod_dead, inc="V")
        ev_slot[ts % 2] = e2
        ev_hv[ts] = v(lambda e, t=ts:
                      e.tensor_add(harena[:, t * D:(t + 1) * D], scrC, sa_b[:]),
                      wait=[ev_sa, e2], inc="V")

    if phases < 3:
        return _finish(nc, p, sems, ctx, dbg, locals())
    ev_qk = []
    ev_strip_free = ev_ptfree[:]
    for tg in range(4):
        ev_last_strip = None
        for i4 in range(4):
            ts = tg * 4 + i4
            for dc in range(NDC):
                bank = (i4 * NDC + dc) % 2
                wv = [ev_hv[ts], ev_strip_free[bank]]
                if tg > 0:
                    wv.append(ev_qk[-1][0])
                evt = t_tr(pT[bank][:, 0:128],
                           harena[:, ts * D + dc * 128:ts * D + (dc + 1) * 128],
                           identb[:], wait=wv, inc="T")
                ev_last_strip = v(
                    lambda e, c=dc, i=i4, b=bank:
                    e.tensor_copy(strip[:, c * 512 + i * 128:c * 512 + (i + 1) * 128],
                                  pT[b][:, 0:128]),
                    wait=[evt], inc="V")
                ev_strip_free[bank] = ev_last_strip
        wv = [ev_last_strip, ev_wqkT, ev_pc_free[0], ev_pc_free[1]]
        if tg > 0:
            wv += [ev_qk[-1][1], ev_qk[-1][2]]
        evq = evk = None
        for dc in range(NDC):
            evq = t_mm(pC[0][0:QK, :], wqT[:, dc * QK:(dc + 1) * QK],
                       strip[:, dc * 512:(dc + 1) * 512],
                       start=(dc == 0), stop=(dc == NDC - 1),
                       wait=wv if dc == 0 else (), inc="T" if dc == NDC - 1 else None)
        for dc in range(NDC):
            evk = t_mm(pC[1][0:QK, :], wkT[:, dc * QK:(dc + 1) * QK],
                       strip[:, dc * 512:(dc + 1) * 512],
                       start=(dc == 0), stop=(dc == NDC - 1),
                       inc="T" if dc == NDC - 1 else None)
        eq = s(lambda e, t=tg: e.activation(qT[0:QK, t * 512:(t + 1) * 512],
                                            pC[0][0:QK, :], AF.Identity,
                                            scale=0.25, bias=scol[0:QK, 0:1]),
               wait=[evq, ev_bq4], inc="S")
        ek = s(lambda e, t=tg: e.activation(kT[0:QK, t * 512:(t + 1) * 512],
                                            pC[1][0:QK, :], AF.Identity,
                                            scale=1.0, bias=scol[0:QK, 1:2]),
               wait=[evk], inc="S")
        ev_qk.append((evk, eq, ek))
    ev_qk_done = ev_qk[-1][2]
    ev_pc_free = [ev_qk[-1][1], ev_qk[-1][2]]

    if phases < 4:
        return _finish(nc, p, sems, ctx, dbg, locals())
    ev_scA = [None, None]
    ev_ptA = ev_strip_free[:]
    ev_pB_free = [None, None]
    ev_att_res = [None] * NTS
    ev_sc_done = {}
    ev_exp = [None] * NTS
    ev_norm = [None] * NTS
    ev_rowmax = [None] * NTS

    def emit_scores(qt):
        L = (qt + 1) * 128
        buf = scb[qt % 2]
        nj = (L + 511) // 512
        evd = None
        for j in range(nj):
            n = min(512, L - j * 512)
            dn = 128 if j == nj - 1 else 0
            bank = (qt + j) % 2
            wv = [ev_qk_done]
            if ev_scA[bank]:
                wv.append(ev_scA[bank])
            evm = t_mm(pA[bank][:, 0:n], qT[0:QK, qt * 128:(qt + 1) * 128],
                       kT[0:QK, j * 512:j * 512 + n],
                       start=True, stop=True, wait=wv, inc="T")
            wv2 = [evm]
            if qt >= 2 and ev_rowmax[qt - 2]:
                wv2.append(ev_rowmax[qt - 2])
            if qt >= 2 and ev_exp[qt - 2]:
                wv2.append(ev_exp[qt - 2])
            ev_s = ev_scA[bank]
            if n - dn > 0:
                ev_s = s(lambda e, b=bank, bu=buf, jj=j, nn=n - dn:
                         e.activation(bu[:, jj * 512:jj * 512 + nn],
                                      pA[b][:, 0:nn], AF.Identity),
                         wait=wv2, inc="S")
            if dn:
                evd = v(lambda e, b=bank, bu=buf, jj=j, nn=n, l=L:
                        e.tensor_tensor(bu[:, l - 128:l],
                                        pA[b][:, nn - 128:nn], cmask[:], A.add),
                        wait=wv2 + [evG] + ([ev_s] if n - dn > 0 else []),
                        inc="V")
            ev_scA[bank] = evd if dn else ev_s
        ev_sc_done[qt] = (ev_scA[0], ev_scA[1], evd)

    def emit_softmax_trans_vmm(qt):
        L = (qt + 1) * 128
        buf = scb[qt % 2]
        wv = list(ev_sc_done[qt])
        if qt >= 1:
            wv.append(ev_exp[qt - 1])
        e2 = vr(lambda e, l=L, bu=buf: e.reduce_max(mx_c[:, 0:1], bu[:, 0:l],
                                                    axis=AX.X),
                wait=wv, inc="V")
        ev_rowmax[qt] = e2
        e3 = v(lambda e: e.tensor_scalar(mx_c[:, 1:2], mx_c[:, 0:1], -1.0, None,
                                         A.mult), wait=[e2], inc="V")
        e4 = s(lambda e, l=L, bu=buf: e.activation(probs[:, 0:l], bu[:, 0:l],
                                                   AF.Exp, bias=mx_c[:, 1:2],
                                                   accum_out=mx_c[:, 2:3]),
               wait=[e3] + ([ev_norm[qt - 1]] if qt >= 1 else []), inc="S")
        ev_exp[qt] = e4
        s(lambda e: e.activation(mx_c[:, 3:4], mx_c[:, 2:3], AF.Ln), inc="S")
        e5 = s(lambda e: e.activation(mx_c[:, 4:5], mx_c[:, 3:4], AF.Exp,
                                      scale=-1.0), inc="S")
        ev_ptev = [None] * (qt + 1)
        for kt in range(qt + 1):
            bank = kt % 2
            evt = t_tr(pT[bank][:, 0:128], probs[:, kt * 128:(kt + 1) * 128],
                       identb[:], wait=[e4, ev_ptA[bank]], inc="T")
            eve = v(lambda e, k=kt, b=bank:
                    e.tensor_copy(probsT[:, k * 128:(k + 1) * 128],
                                  pT[b][:, 0:128]),
                    wait=[evt], inc="V")
            ev_ptA[bank] = eve
            ev_ptev[kt] = eve
        for dh in range(2):
            evm = None
            for kt in range(qt + 1):
                wv2 = [ev_ptev[kt]]
                if kt == 0 and ev_pB_free[dh]:
                    wv2.append(ev_pB_free[dh])
                evm = t_mm(pB[dh][:], probsT[:, kt * 128:(kt + 1) * 128],
                           harena[:, kt * D + dh * 512:kt * D + (dh + 1) * 512],
                           start=(kt == 0), stop=(kt == qt),
                           wait=wv2, inc="T" if kt == qt else None)
            ev1 = v(lambda e, d=dh:
                    e.scalar_tensor_tensor(scrA[:, 0:512], pB[d][:],
                                           mx_c[:, 4:5],
                                           ga_b[:, d * 512:(d + 1) * 512],
                                           A.mult, A.mult),
                    wait=[evm, ev_gab, e5], inc="V")
            ev_pB_free[dh] = ev1
            ev_att_res[qt] = v(lambda e, q=qt, d=dh:
                               e.tensor_add(x_v[:, q, d * 512:(d + 1) * 512],
                                            x_v[:, q, d * 512:(d + 1) * 512],
                                            scrA[:, 0:512]),
                               wait=[ev1], inc="V")
        ev_norm[qt] = ev_att_res[qt]

    naqt = int(os.environ.get("KAQT", str(NTS)))
    emit_scores(0)
    for qt in range(naqt):
        if qt + 1 < naqt:
            emit_scores(qt + 1)
        emit_softmax_trans_vmm(qt)
    if naqt < NTS:
        for qx in range(naqt, NTS):
            ev_att_res[qx] = ev_att_res[naqt - 1]
            ev_exp[qx] = ev_exp[naqt - 1]
            ev_rowmax[qx] = ev_rowmax[naqt - 1]

    if phases < 5:
        return _finish(nc, p, sems, ctx, dbg, locals())
    ev_stat2 = []
    for ts in range(NTS):
        wx = [ev_att_res[ts]]
        e1 = vr(lambda e, t=ts: e.reduce_sum(sum_c[:, t:t + 1], x_v[:, t, :],
                                             axis=AX.X), wait=wx, inc="V")
        e2 = s(lambda e, t=ts: e.activation(scrB, x_v[:, t, :], AF.Square,
                                            accum_out=ssq_c[:, t:t + 1]),
               wait=wx + [ev_exp[NTS - 1]], inc="S")
        ev_stat2.append((e1, e2))
    ev_ln2 = ln_cols(ev_stat2[-1][0], ev_stat2[-1][1])

    ev_h2T = [None] * NTS
    ev_slot = [None, None]
    ev_oslot = [None, None]
    ev_ptL = ev_ptA[:]
    h2slots = [probs[:, 0:1024], probs[:, 1024:2048]]
    for ts in range(NTS):
        slot = [scrA, scrB][ts % 2]
        oslot = h2slots[ts % 2]
        wv = [ev_ln2]
        if ev_slot[ts % 2]:
            wv.append(ev_slot[ts % 2])
        e1 = s(lambda e, t=ts, sl=slot:
               e.activation(sl, x_v[:, t, :], AF.Identity,
                            scale=rs_c[:, t:t + 1], bias=nm_c[:, t:t + 1]),
               wait=wv, inc="S")
        e2 = v(lambda e, sl=slot: e.scalar_tensor_tensor(scrC, cb_b[:], 1.0, sl,
                                                         A.add, A.mult),
               wait=[e1, ev_cb], inc="V")
        ev_slot[ts % 2] = e2
        e3 = v(lambda e, osl=oslot: e.tensor_add(osl, scrC, sb_b[:]),
               wait=[ev_sb2, e2] + ([ev_oslot[ts % 2]] if ev_oslot[ts % 2] else []),
               inc="V")
        last_tr = None
        eve = None
        for dc in range(NDC):
            bank = dc % 2
            last_tr = t_tr(pT[bank][:, 0:128], oslot[:, dc * 128:(dc + 1) * 128],
                           identb[:], wait=[e3, ev_ptL[bank]], inc="T")
            eve = v(lambda e, t=ts, c=dc, b=bank:
                    e.tensor_copy(harena[:, c * S + t * 128:c * S + (t + 1) * 128],
                                  pT[b][:, 0:128]),
                    wait=[last_tr], inc="V")
            ev_ptL[bank] = eve
        ev_oslot[ts % 2] = last_tr
        ev_h2T[ts] = eve
    ev_h2T_all = ev_h2T[-1]

    if phases < 6:
        return _finish(nc, p, sems, ctx, dbg, locals())
    ev_b2d = gd(b2rows, hb2[:, :],
                wait=[ev_att_res[NTS - 1], ev_exp[NTS - 1]], sem="SM")
    ev_b2gb = v(lambda e: e.tensor_tensor(b2gb, b2rows, gb_b[0:8, :], A.mult),
                wait=[ev_b2d, ev_gbb], inc="V")
    ev_sel = [None] * NTS
    ev_selT = [None] * NTS
    ev_gm_prev = [None, None]
    ev_selbf_used = None
    ev_ptG = ev_ptL[:]
    for ts in range(NTS):
        bank = ts % 2
        wv = [ev_h2T_all, evSM, ev_pc_free[bank]]
        evm = None
        for dc in range(NDC):
            evm = t_mm(pC[bank][:, 0:E],
                       harena[:, dc * S + ts * 128:dc * S + (ts + 1) * 128],
                       wg_sb[:, dc * E:(dc + 1) * E],
                       start=(dc == 0), stop=(dc == NDC - 1),
                       wait=wv if dc == 0 else (), inc="T" if dc == NDC - 1 else None)
        ega = v(lambda e, t=ts, b=bank:
                e.tensor_tensor(g_tok[:, t * E:(t + 1) * E], pC[b][:, 0:E],
                                bg_bc[:], A.add),
                wait=[evm, ev_bgbc], inc="V")
        ev_pc_free[bank] = ega
        v_nop()
        gm = tm_c[:, (ts % 2):(ts % 2) + 1]
        em = vr(lambda e, t=ts, g=gm: e.reduce_max(g, g_tok[:, t * E:(t + 1) * E],
                                                   axis=AX.X),
                wait=[ega] + ([ev_gm_prev[ts % 2]] if ev_gm_prev[ts % 2] else []),
                inc="V")
        ev_gm_prev[ts % 2] = em
        ev_sel[ts] = v(lambda e, t=ts, g=gm:
                       e.tensor_scalar(sel[:, t * E:(t + 1) * E],
                                       g_tok[:, t * E:(t + 1) * E], g, None,
                                       A.is_equal),
                       wait=[em], inc="V")
        v_nop()
        evbf = v(lambda e, t=ts: e.tensor_copy(selbf, sel[:, t * E:(t + 1) * E]),
                 wait=[ev_b2d, ev_selbf_used, ev_sel[ts]], inc="V")
        evt2 = t_tr(pT[ts % 2][0:8, 0:128], selbf, identb[:],
                    wait=[evbf, ev_ptG[ts % 2]], inc="T")
        ev_selbf_used = evt2
        ev_selT[ts] = v(lambda e, t=ts, b=ts % 2:
                        e.tensor_copy(selT[0:8, t * 128:(t + 1) * 128],
                                      pT[b][0:8, 0:128]),
                        wait=[evt2], inc="V")
        ev_ptG[ts % 2] = ev_selT[ts]

    if phases < 7:
        return _finish(nc, p, sems, ctx, dbg, locals())
    w1_r = [hw1[e].rearrange("(c q) h -> q c h", q=128) for e in range(E)]
    w2_r = [hw2[e].rearrange("(a c q) d -> q a c d", a=NQ, c=HCQ) for e in range(E)]
    w1b_full = [w1b[i][:].rearrange("q (c m) -> q c m", c=NDC) for i in range(2)]
    w2b_full = [w2b[i][:].rearrange("q (c m) -> q c m", c=HCQ) for i in range(2)]

    ev_wq_cons = [ev_modmm[10], ev_modmm[11]]
    ev_pA_free = ev_scA[:]
    ev_pB_free2 = ev_pB_free[:]
    ev_outsb_S = [None] * NTS
    ev_gating = [None] * NTS
    ev_mm2_done_prev = None

    nfe = int(os.environ.get("KFE", str(E)))
    for g in range(nfe * NQ):
        e_idx, Q = divmod(g, NQ)
        par = g % 2
        wv = [ev_wq_cons[par]] if ev_wq_cons[par] else []
        evw1 = gd(w1b_full[par], w1_r[e_idx][:, :, Q * D:(Q + 1) * D],
                  wait=wv, sem="W")
        evw2 = gd(w2b_full[par], w2_r[e_idx][:, Q], sem="W")
        ev_mm2_t_done = None
        for tc in range(4):
            ev_h1T = [None] * HCQ
            for hcl in range(HCQ):
                bank = (tc * HCQ + hcl) % 2
                wv = [evw1, ev_h2T_all]
                if ev_pA_free[bank]:
                    wv.append(ev_pA_free[bank])
                evm = None
                for dc in range(NDC):
                    evm = t_mm(pA[bank][:],
                               w1b[par][:, dc * D + hcl * 128:dc * D + (hcl + 1) * 128],
                               harena[:, dc * S + tc * 512:dc * S + (tc + 1) * 512],
                               start=(dc == 0), stop=(dc == NDC - 1),
                               wait=wv if dc == 0 else (),
                               inc="T" if dc == NDC - 1 else None)
                hc = Q * HCQ + hcl
                wv2 = [evm, ev_b1_done]
                if tc > 0:
                    wv2.append(ev_mm2_t_done)
                elif ev_mm2_done_prev:
                    wv2.append(ev_mm2_done_prev)
                ev_h1T[hcl] = s(lambda e, hl=hcl, h=hc, ei=e_idx, b=bank:
                                e.activation(h1T[:, hl * 512:(hl + 1) * 512],
                                             pA[b][:], AF.Relu,
                                             bias=b1c[:, h * E + ei:h * E + ei + 1]),
                                wait=wv2, inc="S")
                ev_pA_free[bank] = ev_h1T[hcl]
            for ts_l in range(4):
                ts_g = tc * 4 + ts_l
                for dh in range(2):
                    bank = (ts_l * 2 + dh) % 2
                    wv = [evw2]
                    if ev_pB_free2[bank]:
                        wv.append(ev_pB_free2[bank])
                    evm = None
                    for hcl in range(HCQ):
                        evm = t_mm(pB[bank][:],
                                   h1T[:, hcl * 512 + ts_l * 128:hcl * 512 + (ts_l + 1) * 128],
                                   w2b[par][:, hcl * D + dh * 512:hcl * D + (dh + 1) * 512],
                                   start=(hcl == 0), stop=(hcl == HCQ - 1),
                                   wait=(wv + [ev_h1T[hcl]]) if hcl == 0 else [ev_h1T[hcl]],
                                   inc="T" if hcl == HCQ - 1 else None)
                    ev_mm2_t_done = evm
                    osl = out_sb[:, ts_g * D + dh * 512:ts_g * D + (dh + 1) * 512]
                    if Q == 0:
                        wv3 = [evm]
                        if e_idx > 0 and ev_gating[ts_g]:
                            wv3.append(ev_gating[ts_g])
                        eve = s(lambda e, o=osl, b=bank:
                                e.activation(o, pB[b][:], AF.Identity),
                                wait=wv3, inc="S")
                        ev_outsb_S[ts_g] = eve
                    else:
                        eve = v(lambda e, o=osl, b=bank:
                                e.tensor_add(o, o, pB[b][:]),
                                wait=[evm, ev_outsb_S[ts_g]], inc="V")
                    ev_pB_free2[bank] = eve
        ev_wq_cons[par] = ev_mm2_t_done
        ev_mm2_done_prev = ev_mm2_t_done
        if Q == NQ - 1:
            for ts in range(NTS):
                evst = v(lambda e, t=ts, ei=e_idx:
                         e.scalar_tensor_tensor(scrA, out_sb[:, t * D:(t + 1) * D],
                                                sel[:, t * E + ei:t * E + ei + 1],
                                                gb_b[:], A.mult, A.mult),
                         wait=[ev_sel[ts], ev_gbb, ev_pB_free2[0],
                               ev_pB_free2[1]], inc="V")
                ev_gating[ts] = v(lambda e, t=ts:
                                  e.tensor_add(x_v[:, t, :], x_v[:, t, :], scrA),
                                  wait=[evst], inc="V")

    if nfe < E:
        for ts in range(NTS):
            if ev_gating[ts] is None:
                ev_gating[ts] = ("V", 0)
    ev_final = [None] * NTS
    for ts in range(NTS):
        for dh in range(2):
            bank = (ts * 2 + dh) % 2
            wv = [ev_selT[ts], ev_b2gb, ev_gating[ts], ev_pc_free[bank]]
            evm = t_mm(pC[bank][:], selT[0:8, ts * 128:(ts + 1) * 128],
                       b2gb[0:8, dh * 512:(dh + 1) * 512],
                       start=True, stop=True, wait=wv, inc="T")
            ev_final[ts] = v(lambda e, t=ts, d=dh, b=bank:
                             e.tensor_add(x_v[:, t, d * 512:(d + 1) * 512],
                                          x_v[:, t, d * 512:(d + 1) * 512],
                                          pC[b][:]),
                             wait=[evm], inc="V")
            ev_pc_free[bank] = ev_final[ts]

    return _finish(nc, p, sems, ctx, dbg, locals())


def _finish(nc, p, sems, ctx, dbg, L):
    gd = L["gd"]
    x_v, hout = L["x_v"], L["hout"]
    n_out = 0
    if L.get("ev_final") and L["ev_final"][0] is not None:
        out_r = hout.rearrange("(ts q) d -> q ts d", q=128)
        for i in range(4):
            wv = [L["ev_final"][i * 4 + k] for k in range(4)]
            gd(out_r[:, i * 4:(i + 1) * 4, :], x_v[:, i * 4:(i + 1) * 4, :],
               wait=wv, sem="O")
        n_out = 4
    dbg_srcs = {
        "mod": L["mod_row"], "hv": L["harena"][:, 0:2048],
        "xattn": L["x_sb"][:, 0:2048], "qT": L["qT"][0:QK, 0:512],
        "kT": L["kT"][0:QK, 0:512], "probs0": L["probs"][:, 0:1024],
        "h2T": L["harena"][:, 0:2048], "gtok": L["g_tok"][:],
        "sel": L["sel"][:], "outsb": L["out_sb"][:, 0:2048],
        "cab": L["ca_b"][:], "sab": L["sa_b"][:], "gab": L["ga_b"][:],
        "gbb": L["gb_b"][:], "b1c": L["b1c"][:, 0:256],
        "h1T": L["h1T"][:, 0:2048], "x0": L["x_sb"][:, 0:2048],
        "scores": L["scb"][1][:, 0:2048], "mxc": L["mx_c"][:], "sumc": L["sum_c"][:],
        "idxf": L["idxf"][:], "max8": L["max8"][:], "xall": L["x_sb"][:], "selTd": L["selT"], "b2gbd": L["b2gb"],
        "ssqc": L["ssq_c"][:], "mc": L["m_c"][:], "rsc": L["rs_c"][:],
        "nmc": L["nm_c"][:],
    }
    for nm in dbg:
        src_ap = dbg_srcs[nm]
        hd = nc.dram_tensor("dbg_" + nm, list(src_ap.shape), F32,
                            kind="ExternalOutput").ap()
        gd(hd[:, :], src_ap,
           wait=[("V", p.ctr.get("V", 0)), ("S", p.ctr.get("S", 0)),
                 ("T", p.ctr.get("T", 0)), ("X", p.ctr.get("X", 0))], sem="O")
        n_out += 1
    p.on("gpsimd", lambda g: None, wait=[("O", 16 * n_out)])
    p.on("sync", lambda sy: None, wait=[("O", 16 * n_out)])

    block = nc.Block()
    with block:
        @block.tensor
        def _(t):
            p.emit("tensor", t, sems)

        @block.vector
        def _(e):
            p.emit("vector", e, sems)

        @block.scalar
        def _(e):
            p.emit("scalar", e, sems)

        @block.sync
        def _(e):
            p.emit("sync", e, sems)

        @block.gpsimd
        def _(e):
            p.emit("gpsimd", e, sems)
    return nc


_NC_CACHE = {}


def _get_nc(dbg=(), phases=99):
    key = (tuple(dbg), phases)
    if key not in _NC_CACHE:
        _NC_CACHE[key] = build_nc(tuple(dbg), phases)
    return _NC_CACHE[key]


def kernel(L0_attn_x, L0_attn_w_q, L0_attn_b_q, L0_attn_w_k, L0_attn_b_k,
           L0_ffn_Wg, L0_ffn_bg, L0_ffn_W1, L0_ffn_b1, L0_ffn_W2, L0_ffn_b2,
           L0_c, L0_wmod, _dbg=(), _phases=99):
    nc = _get_nc(_dbg, _phases)
    f32 = np.float32
    asc = np.ascontiguousarray

    def arr(a):
        return asc(np.asarray(a, dtype=f32))

    shared = {
        "w_q": arr(L0_attn_w_q), "b_q": arr(L0_attn_b_q).reshape(QK, 1),
        "w_k": arr(L0_attn_w_k), "b_k": arr(L0_attn_b_k).reshape(QK, 1),
        "wg": arr(L0_ffn_Wg), "bg_r": arr(L0_ffn_bg).reshape(1, E),
        "w1": arr(L0_ffn_W1), "b1": arr(L0_ffn_b1),
        "w2": arr(L0_ffn_W2), "b2": arr(L0_ffn_b2),
        "wmod": arr(L0_wmod),
    }
    x = np.asarray(L0_attn_x, dtype=f32)
    c = np.asarray(L0_c, dtype=f32)
    in_maps = []
    for i in range(8):
        m = dict(shared)
        m["x_in"] = asc(x[i])
        m["cvec"] = asc(c[i:i + 1])
        in_maps.append(m)
    res = run_bass_kernel_spmd(nc, in_maps, core_ids=list(range(8)))
    if _phases < 99:
        return None, res
    out = np.stack([r["out"] for r in res.results], axis=0)
    if _dbg:
        return out, res
    return out


# version 02 (final)
# speedup vs baseline: 1.0023x; 1.0023x over previous
import contextlib
import os
import numpy as np

import concourse.bass as bass
import concourse.mybir as mybir
from concourse.bass_utils import run_bass_kernel_spmd

F32 = mybir.dt.float32
BF16 = mybir.dt.bfloat16
U32 = mybir.dt.uint32
A = mybir.AluOpType
AF = mybir.ActivationFunctionType
AX = mybir.AxisListType

S = 2048
D = 1024
H = 4096
E = 8
QK = 16
NTS = S // 128
NDC = D // 128
NHC = H // 128
NQ = 4
HCQ = NHC // NQ
EPS = 1e-5
MASK_ADD = -30000.0


class Prog:
    ENGINES = ("tensor", "vector", "scalar", "sync", "gpsimd")

    def __init__(self):
        self.ops = {e: [] for e in self.ENGINES}
        self.ctr = {}

    def bump(self, sem, n=1):
        self.ctr[sem] = self.ctr.get(sem, 0) + n
        return self.ctr[sem]

    def on(self, eng, fn, wait=(), inc=None, inc_n=1):
        ev = (inc, self.bump(inc, inc_n)) if inc is not None else None
        self.ops[eng].append((fn, [w for w in wait if w], inc, inc_n))
        return ev

    def emit(self, name, eng, sems):
        last = {}
        for fn, waits, inc, inc_n in self.ops[name]:
            for sm, val in waits:
                if val > 0 and val > last.get(sm, 0):
                    eng.wait_ge(sems[sm], val)
                    last[sm] = val
            ins = fn(eng)
            if inc is not None:
                if ins is None:
                    ins = eng.drain()
                ins.then_inc(sems[inc], inc_n)


def build_nc(dbg=(), phases=99):
    nc = bass.Bass("TRN2", target_bir_lowering=False,
                   dynamic_dma_scratch_size=16384)
    ctx = contextlib.ExitStack()

    hx = nc.dram_tensor("x_in", [S, D], F32, kind="ExternalInput").ap()
    hwq = nc.dram_tensor("w_q", [QK, D], F32, kind="ExternalInput").ap()
    hbq = nc.dram_tensor("b_q", [QK, 1], F32, kind="ExternalInput").ap()
    hwk = nc.dram_tensor("w_k", [QK, D], F32, kind="ExternalInput").ap()
    hbk = nc.dram_tensor("b_k", [QK, 1], F32, kind="ExternalInput").ap()
    hwg = nc.dram_tensor("wg", [D, E], F32, kind="ExternalInput").ap()
    hbg = nc.dram_tensor("bg_r", [1, E], F32, kind="ExternalInput").ap()
    hw1 = nc.dram_tensor("w1", [E, D, H], F32, kind="ExternalInput").ap()
    hb1 = nc.dram_tensor("b1", [E, H], F32, kind="ExternalInput").ap()
    hw2 = nc.dram_tensor("w2", [E, H, D], F32, kind="ExternalInput").ap()
    hb2 = nc.dram_tensor("b2", [E, D], F32, kind="ExternalInput").ap()
    hc = nc.dram_tensor("cvec", [1, D], F32, kind="ExternalInput").ap()
    hwmod = nc.dram_tensor("wmod", [D, 6 * D], F32, kind="ExternalInput").ap()
    hout = nc.dram_tensor("out", [S, D], F32, kind="ExternalOutput").ap()

    sbuf = lambda name, shape, dt: ctx.enter_context(nc.sbuf_tensor(name, shape, dt))
    x_sb = sbuf("x_sb", [128, NTS * D], BF16)
    harena = sbuf("harena", [128, NTS * D], BF16)
    out_sb = sbuf("outsb", [128, NTS * D], BF16)
    w1b = [sbuf(f"w1b{i}", [128, NDC * D], BF16) for i in range(2)]
    w2b = [sbuf(f"w2b{i}", [128, HCQ * D], BF16) for i in range(2)]
    scb = [sbuf(f"scb{i}", [128, S], BF16) for i in range(2)]
    ascr = sbuf("ascr", [128, 6144], BF16)
    h1T = sbuf("h1T", [128, HCQ * 512], BF16)
    ca_b = sbuf("ca_b", [128, D], BF16)
    cb_b = sbuf("cb_b", [128, D], BF16)
    sb_b = sbuf("sb_b", [128, D], BF16)
    ga_b = sbuf("ga_b", [128, D], BF16)
    gb_b = sbuf("gb_b", [128, D], BF16)
    wqT = sbuf("wqT", [128, NDC * QK], BF16)
    wkT = sbuf("wkT", [128, NDC * QK], BF16)
    wg_sb = sbuf("wg_sb", [128, NDC * E], BF16)
    b1c = sbuf("b1c", [128, NHC * E], F32)
    sel = sbuf("sel", [128, NTS * E], F32)
    g_tok = sbuf("g_tok", [128, NTS * E], F32)
    scol = sbuf("scol", [128, 2], F32)
    bgrow = sbuf("bgrow", [1, E], BF16)
    bg_bc = sbuf("bg_bc", [128, E], F32)
    idx8 = sbuf("idx8", [128, 8], U32)
    idxf = sbuf("idxf", [128, NTS], F32)
    max8 = sbuf("max8", [128, 8], F32)
    iotaE = sbuf("iotaE", [128, E], F32)
    identb = sbuf("identb", [128, 128], BF16)
    cmask = sbuf("cmask", [128, 128], BF16)
    ones_r = sbuf("ones_r", [1, 128], BF16)
    sc_cols = sbuf("sc_cols", [128, NDC], BF16)
    sum_c = sbuf("sum_c", [128, NTS], F32)
    ssq_c = sbuf("ssq_c", [128, NTS], F32)
    m_c = sbuf("m_c", [128, NTS], F32)
    rs_c = sbuf("rs_c", [128, NTS], F32)
    nm_c = sbuf("nm_c", [128, NTS], F32)
    tm_c = sbuf("tm_c", [128, NTS], F32)
    mx_c = sbuf("mx_c", [128, 8], F32)

    x_v = x_sb[:].rearrange("q (ts d) -> q ts d", ts=NTS)
    qT = out_sb[0:QK, 0:S]
    kT = out_sb[0:QK, S:2 * S]
    c_row = scb[0][0:1, 0:D]
    scrow = scb[1][0:1, 0:D]
    b2rows = scb[0][0:8, 0:D]
    b2gb = scb[1][0:8, 0:D]
    selbf = scb[0][:, D:D + 8]
    mod_row = ascr[0:1, :]
    t_abs = kT[0:1, 0:D]
    t_sgn = qT[0:1, 0:D]
    b1rows = ascr[32:40, 0:D]
    wq_rows = ascr[64:80, 2 * D:3 * D]
    wk_rows = ascr[64:80, 3 * D:4 * D]
    strip = ascr[:, 0:4096]
    probs = ascr[:, 0:2048]
    probsT = ascr[:, 2048:4096]
    selT = ascr[0:8, 0:2048]
    scrA = ascr[:, 4096:5120]
    scrB = ascr[:, 5120:6144]
    scrC = ascr[:, 0:1024]
    sa_b = ascr[:, 1024:2048]

    psum = lambda name, shape, dt: ctx.enter_context(nc.psum_tensor(name, shape, dt))
    pA = [psum("pA0", [128, 512], F32), psum("pA1", [128, 512], F32)]
    pB = [psum("pB0", [128, 512], F32), psum("pB1", [128, 512], F32)]
    pT = [psum("pT0", [128, 1024], BF16), psum("pT1", [128, 1024], BF16)]
    pC = [psum("pC0", [128, 512], F32), psum("pC1", [128, 512], F32)]

    sems = {n: ctx.enter_context(nc.semaphore(n))
            for n in ["SM", "WM", "X", "W", "O", "T", "V", "S", "G"]}
    p = Prog()

    def t_mm(out, lhsT, rhs, start, stop, wait=(), inc=None):
        return p.on("tensor",
                    lambda t, o=out, l=lhsT, r=rhs, s1=start, s2=stop:
                    t.matmul(o, l, r, start=s1, stop=s2), wait=wait, inc=inc)

    def t_tr(out, in_, idn, wait=(), inc=None):
        return p.on("tensor", lambda t, o=out, i=in_, d=idn: t.transpose(o, i, d),
                    wait=wait, inc=inc)

    def v(fn, wait=(), inc=None):
        return p.on("vector", fn, wait=wait, inc=inc)

    def vr(fn, wait=(), inc=None):
        def g(eng, f=fn):
            f(eng)
            return eng.engine_nop()
        return p.on("vector", g, wait=wait, inc=inc)

    def s(fn, wait=(), inc=None):
        return p.on("scalar", fn, wait=wait, inc=inc)

    def v_nop():
        p.on("vector", lambda e: e.engine_nop())

    def s_table6():
        def f(e):
            return e.add_instruction(mybir.InstLoadActFuncSet(
                name=e.bass.get_next_instruction_name(), act_func_set_id=6,
                ins=[], outs=[]))
        return p.on("scalar", f)

    def gd(out, in_, wait=(), sem="W", slow=False):
        return p.on("gpsimd",
                    lambda g, o=out, i=in_, sl=slow:
                    g.dma_start(out=o, in_=i, allow_slow_non_contiguous=sl),
                    wait=wait, inc=sem, inc_n=16)

    def g_setup(g):
        g.memset(identb[:], 0.0)
        g.affine_select(out=identb[:], in_=identb[:], compare_op=A.not_equal,
                        fill=1.0, base=0, pattern=[[-1, 128]], channel_multiplier=1)
        g.memset(cmask[:], 0.0)
        g.affine_select(out=cmask[:], in_=cmask[:], compare_op=A.is_ge,
                        fill=MASK_ADD, base=0, pattern=[[-1, 128]], channel_multiplier=1)
        g.iota(iotaE[:], pattern=[[1, E]], base=0, channel_multiplier=0,
               allow_small_or_imprecise_dtypes=True)
        return g.memset(ones_r[:], 1.0)
    evG = p.on("gpsimd", g_setup, inc="G")

    gd(c_row[:], hc[:, :], sem="SM")
    gd(wq_rows, hwq[:, :], sem="SM")
    gd(wk_rows, hwk[:, :], sem="SM")
    gd(scol[0:QK, 0:1], hbq[:, :], sem="SM", slow=True)
    gd(scol[0:QK, 1:2], hbk[:, :], sem="SM", slow=True)
    gd(bgrow[:], hbg[:, :], sem="SM")
    gd(wg_sb[:].rearrange("q (c e) -> q c e", c=NDC),
       hwg.rearrange("(c q) e -> q c e", q=128), sem="SM")
    evSM = ("SM", p.ctr["SM"])

    msub = int(os.environ.get("KMSUB", "99"))
    ev_final = [None] * NTS
    if msub < 1:
        return _finish(nc, p, sems, ctx, dbg, locals())
    s(lambda e: e.preload_act_table(AF.Silu))
    evSilu = s(lambda e: e.activation(scrow[:], c_row[:], AF.Silu),
               wait=[evSM], inc="S")
    ev_sc = []
    for dc in range(NDC):
        wv = [evSilu, evG]
        if dc >= 2:
            wv.append(ev_sc[dc - 2][1])
        evt = t_tr(pT[dc % 2][:, 0:1], scrow[0:1, dc * 128:(dc + 1) * 128],
                   identb[0:1, 0:1], wait=wv, inc="T")
        eve = v(lambda e, d=dc, b=dc % 2: e.tensor_copy(sc_cols[:, d:d + 1],
                                                        pT[b][:, 0:1]),
                wait=[evt], inc="V")
        ev_sc.append((evt, eve))
    ev_sc_done = ev_sc[-1][1]

    if msub < 2:
        return _finish(nc, p, sems, ctx, dbg, locals())
    NMB = 12
    wmod_r = hwmod.rearrange("(c q) m -> q c m", q=128)
    w1b_v = [w1b[i][:, 0:NDC * 512].rearrange("q (c m) -> q c m", c=NDC)
             for i in range(2)]
    ev_modmm = [None] * NMB
    ev_modev = [None] * NMB
    for nb in range(NMB):
        par = nb % 2
        wv = [evG]
        if nb >= 2:
            wv.append(ev_modmm[nb - 2])
        evd = gd(w1b_v[par], wmod_r[:, :, nb * 512:(nb + 1) * 512], wait=wv, sem="WM")
        mmw = [evd, ev_sc_done]
        if nb >= 2:
            mmw.append(ev_modev[nb - 2])
        evm = None
        for dc in range(NDC):
            evm = t_mm(pA[par][0:1, :], sc_cols[:, dc:dc + 1],
                       w1b[par][:, dc * 512:(dc + 1) * 512],
                       start=(dc == 0), stop=(dc == NDC - 1),
                       wait=mmw if dc == 0 else (),
                       inc="T" if dc == NDC - 1 else None)
        ev_modmm[nb] = evm
        ev_modev[nb] = v(lambda e, n=nb, b=par:
                         e.tensor_copy(mod_row[0:1, n * 512:(n + 1) * 512],
                                       pA[b][0:1, :]),
                         wait=[evm], inc="V")

    if msub < 3:
        return _finish(nc, p, sems, ctx, dbg, locals())
    x_r = hx.rearrange("(ts q) d -> q ts d", q=128)
    ev_x = [gd(x_v[:, i * 4:(i + 1) * 4, :], x_r[:, i * 4:(i + 1) * 4, :], sem="X")
            for i in range(4)]

    if msub < 4:
        return _finish(nc, p, sems, ctx, dbg, locals())
    def clamp(col0, wv):
        gsl = mod_row[0:1, col0:col0 + D]
        e1 = s(lambda e, g=gsl: e.activation(t_abs, g, AF.Abs), wait=wv, inc="S")
        v(lambda e: e.tensor_scalar(t_abs, t_abs, 0.25, None, A.max),
          wait=[e1], inc="V")
        v(lambda e, g=gsl: e.tensor_scalar(t_sgn, g, 0.0, None, A.is_ge),
          wait=wv, inc="V")
        v_nop()
        v(lambda e: e.tensor_scalar(t_sgn, t_sgn, 2.0, -1.0, A.mult, A.add), inc="V")
        v_nop()
        return v(lambda e, g=gsl: e.tensor_tensor(g, t_sgn, t_abs, A.mult), inc="V")
    ev_ga = clamp(2 * D, [ev_modev[4], ev_modev[5]])
    ev_gb = clamp(5 * D, [ev_modev[10], ev_modev[11], ev_ga])

    if msub < 5:
        return _finish(nc, p, sems, ctx, dbg, locals())
    ev_pc_free = [None, None]
    def bcast(vec, dst, extra=()):
        eve = None
        for j in range(2):
            bank = j % 2
            wv = [ev_modev[vec * 2 + j]] + list(extra)
            if ev_pc_free[bank]:
                wv.append(ev_pc_free[bank])
            evm = t_mm(pC[bank][:], ones_r[:],
                       mod_row[0:1, vec * D + j * 512:vec * D + (j + 1) * 512],
                       start=True, stop=True, wait=wv, inc="T")
            eve = v(lambda e, d=dst, jj=j, b=bank:
                    e.tensor_copy(d[:, jj * 512:(jj + 1) * 512], pC[b][:]),
                    wait=[evm], inc="V")
            ev_pc_free[bank] = eve
        return eve
    ev_ca = bcast(1, ca_b)
    ev_sa = bcast(0, sa_b)
    ev_gab = bcast(2, ga_b, extra=[ev_ga])
    ev_sb2 = bcast(3, sb_b)
    ev_cb = bcast(4, cb_b)
    ev_gbb = bcast(5, gb_b, extra=[ev_gb])

    if msub < 6:
        return _finish(nc, p, sems, ctx, dbg, locals())
    wv = [evSM, ev_pc_free[0]]
    evm = t_mm(pC[0][:, 0:E], ones_r[:], bgrow[:], start=True, stop=True,
               wait=wv, inc="T")
    ev_bgbc = v(lambda e: e.tensor_copy(bg_bc[:], pC[0][:, 0:E]),
                wait=[evm], inc="V")
    ev_pc_free[0] = ev_bgbc

    ev_b2gb = None
    for j in range(2):
        bank = j % 2
        wv = [ev_gb, evSM]
        if ev_pc_free[bank]:
            wv.append(ev_pc_free[bank])
        evm = t_mm(pC[bank][0:8, :], ones_r[0:1, 0:8],
                   mod_row[0:1, 5 * D + j * 512:5 * D + (j + 1) * 512],
                   start=True, stop=True, wait=wv, inc="T")
        ev_b2gb = v(lambda e, jj=j, b=bank:
                    e.tensor_tensor(b2gb[0:8, jj * 512:(jj + 1) * 512],
                                    b2rows[0:8, jj * 512:(jj + 1) * 512],
                                    pC[b][0:8, :], A.mult),
                    wait=[evm], inc="V")
        ev_pc_free[bank] = ev_b2gb

    if msub < 7:
        return _finish(nc, p, sems, ctx, dbg, locals())
    ev_ptfree = [ev_sc[-2][1], ev_sc[-1][1]]
    ev_wqkT = None
    for i, (rows, dstT) in enumerate(((wq_rows, wqT), (wk_rows, wkT))):
        for dc in range(NDC):
            bank = (i * NDC + dc) % 2
            wv = [evSM, evG, ev_ptfree[bank]]
            evt = t_tr(pT[bank][:, 0:QK], rows[0:QK, dc * 128:(dc + 1) * 128],
                       identb[64:64 + QK, 64:64 + QK], wait=wv, inc="T")
            ev_wqkT = v(lambda e, d=dstT, c=dc, b=bank:
                        e.tensor_copy(d[:, c * QK:(c + 1) * QK], pT[b][:, 0:QK]),
                        wait=[evt], inc="V")
            ev_ptfree[bank] = ev_wqkT
    ev_bq4 = v(lambda e: e.tensor_scalar(scol[0:QK, 0:1], scol[0:QK, 0:1],
                                         0.25, None, A.mult), wait=[evSM], inc="V")

    if msub < 8:
        return _finish(nc, p, sems, ctx, dbg, locals())
    ev_b1t = []
    for q in range(NQ):
        wv = [evSM]
        if q >= 1:
            wv.append(ev_b1t[-1][0])
        evd = gd(b1rows, hb1[:, q * D:(q + 1) * D], wait=wv, sem="SM")
        for hcl in range(HCQ):
            hc = q * HCQ + hcl
            bank = hc % 2
            wv2 = [evd, ev_ptfree[bank]]
            evt = t_tr(pT[bank][:, 0:8], b1rows[0:8, hcl * 128:(hcl + 1) * 128],
                       identb[32:40, 32:40], wait=wv2, inc="T")
            eve = v(lambda e, h=hc, b=bank:
                    e.tensor_copy(b1c[:, h * E:(h + 1) * E], pT[b][:, 0:8]),
                    wait=[evt], inc="V")
            ev_b1t.append((evt, eve))
            ev_ptfree[bank] = eve
    ev_b1_done = ev_b1t[-1][1]

    if phases < 2:
        return _finish(nc, p, sems, ctx, dbg, locals())
    s_table6()
    ev_mod_dead = [ev_gbb, ("T", ev_b1t[-1][0][1])]
    ev_stat = []
    for ts in range(NTS):
        wx = [ev_x[ts // 4]]
        e1 = vr(lambda e, t=ts: e.reduce_sum(sum_c[:, t:t + 1], x_v[:, t, :],
                                             axis=AX.X), wait=wx, inc="V")
        e2 = s(lambda e, t=ts: e.activation(scrB, x_v[:, t, :], AF.Square,
                                            accum_out=ssq_c[:, t:t + 1]),
               wait=wx + ev_mod_dead, inc="S")
        ev_stat.append((e1, e2))

    def ln_cols(ev_sum, ev_ssq):
        v(lambda e: e.tensor_scalar(m_c[:], sum_c[:], 1.0 / D, None, A.mult),
          wait=[ev_sum], inc="V")
        v(lambda e: e.tensor_scalar(rs_c[:], ssq_c[:], 1.0 / D, None, A.mult),
          wait=[ev_ssq], inc="V")
        v_nop()
        v(lambda e: e.tensor_tensor(nm_c[:], m_c[:], m_c[:], A.mult), inc="V")
        v_nop()
        v(lambda e: e.tensor_sub(rs_c[:], rs_c[:], nm_c[:]), inc="V")
        v_nop()
        evv = v(lambda e: e.tensor_scalar(rs_c[:], rs_c[:], EPS, None, A.add),
                inc="V")
        s(lambda e: e.activation(tm_c[:], rs_c[:], AF.Ln), wait=[evv], inc="S")
        evs = s(lambda e: e.activation(rs_c[:], tm_c[:], AF.Exp, scale=-0.5),
                inc="S")
        v(lambda e: e.tensor_tensor(nm_c[:], m_c[:], rs_c[:], A.mult),
          wait=[evs], inc="V")
        v_nop()
        return v(lambda e: e.tensor_scalar(nm_c[:], nm_c[:], -1.0, None, A.mult),
                 inc="V")
    ev_ln1 = ln_cols(ev_stat[-1][0], ev_stat[-1][1])

    ev_hv = [None] * NTS
    ev_slot = [None, None]
    for ts in range(NTS):
        slot = [scrA, scrB][ts % 2]
        wv = [ev_ln1]
        if ev_slot[ts % 2]:
            wv.append(ev_slot[ts % 2])
        e1 = s(lambda e, t=ts, sl=slot:
               e.activation(sl, x_v[:, t, :], AF.Identity,
                            scale=rs_c[:, t:t + 1], bias=nm_c[:, t:t + 1]),
               wait=wv + ev_mod_dead, inc="S")
        e2 = v(lambda e, sl=slot: e.scalar_tensor_tensor(scrC, ca_b[:], 1.0, sl,
                                                         A.add, A.mult),
               wait=[e1, ev_ca] + ev_mod_dead, inc="V")
        ev_slot[ts % 2] = e2
        ev_hv[ts] = v(lambda e, t=ts:
                      e.tensor_add(harena[:, t * D:(t + 1) * D], scrC, sa_b[:]),
                      wait=[ev_sa, e2], inc="V")

    if phases < 3:
        return _finish(nc, p, sems, ctx, dbg, locals())
    ev_qk = []
    ev_strip_free = ev_ptfree[:]
    for tg in range(4):
        ev_last_strip = None
        for i4 in range(4):
            ts = tg * 4 + i4
            for dc in range(NDC):
                bank = (i4 * NDC + dc) % 2
                wv = [ev_hv[ts], ev_strip_free[bank]]
                if tg > 0:
                    wv.append(ev_qk[-1][0])
                evt = t_tr(pT[bank][:, 0:128],
                           harena[:, ts * D + dc * 128:ts * D + (dc + 1) * 128],
                           identb[:], wait=wv, inc="T")
                ev_last_strip = v(
                    lambda e, c=dc, i=i4, b=bank:
                    e.tensor_copy(strip[:, c * 512 + i * 128:c * 512 + (i + 1) * 128],
                                  pT[b][:, 0:128]),
                    wait=[evt], inc="V")
                ev_strip_free[bank] = ev_last_strip
        wv = [ev_last_strip, ev_wqkT, ev_pc_free[0], ev_pc_free[1]]
        if tg > 0:
            wv += [ev_qk[-1][1], ev_qk[-1][2]]
        evq = evk = None
        for dc in range(NDC):
            evq = t_mm(pC[0][0:QK, :], wqT[:, dc * QK:(dc + 1) * QK],
                       strip[:, dc * 512:(dc + 1) * 512],
                       start=(dc == 0), stop=(dc == NDC - 1),
                       wait=wv if dc == 0 else (), inc="T" if dc == NDC - 1 else None)
        for dc in range(NDC):
            evk = t_mm(pC[1][0:QK, :], wkT[:, dc * QK:(dc + 1) * QK],
                       strip[:, dc * 512:(dc + 1) * 512],
                       start=(dc == 0), stop=(dc == NDC - 1),
                       inc="T" if dc == NDC - 1 else None)
        eq = s(lambda e, t=tg: e.activation(qT[0:QK, t * 512:(t + 1) * 512],
                                            pC[0][0:QK, :], AF.Identity,
                                            scale=0.25, bias=scol[0:QK, 0:1]),
               wait=[evq, ev_bq4], inc="S")
        ek = s(lambda e, t=tg: e.activation(kT[0:QK, t * 512:(t + 1) * 512],
                                            pC[1][0:QK, :], AF.Identity,
                                            scale=1.0, bias=scol[0:QK, 1:2]),
               wait=[evk], inc="S")
        ev_qk.append((evk, eq, ek))
    ev_qk_done = ev_qk[-1][2]
    ev_pc_free = [ev_qk[-1][1], ev_qk[-1][2]]

    if phases < 4:
        return _finish(nc, p, sems, ctx, dbg, locals())
    ev_scA = [None, None]
    ev_ptA = ev_strip_free[:]
    ev_pB_free = [None, None]
    ev_att_res = [None] * NTS
    ev_sc_done = {}
    ev_exp = [None] * NTS
    ev_norm = [None] * NTS
    ev_rowmax = [None] * NTS

    def emit_scores(qt):
        L = (qt + 1) * 128
        buf = scb[qt % 2]
        nj = (L + 511) // 512
        evd = None
        for j in range(nj):
            n = min(512, L - j * 512)
            dn = 128 if j == nj - 1 else 0
            bank = (qt + j) % 2
            wv = [ev_qk_done]
            if ev_scA[bank]:
                wv.append(ev_scA[bank])
            evm = t_mm(pA[bank][:, 0:n], qT[0:QK, qt * 128:(qt + 1) * 128],
                       kT[0:QK, j * 512:j * 512 + n],
                       start=True, stop=True, wait=wv, inc="T")
            wv2 = [evm]
            if qt >= 2 and ev_rowmax[qt - 2]:
                wv2.append(ev_rowmax[qt - 2])
            if qt >= 2 and ev_exp[qt - 2]:
                wv2.append(ev_exp[qt - 2])
            ev_s = ev_scA[bank]
            if n - dn > 0:
                ev_s = s(lambda e, b=bank, bu=buf, jj=j, nn=n - dn:
                         e.activation(bu[:, jj * 512:jj * 512 + nn],
                                      pA[b][:, 0:nn], AF.Identity),
                         wait=wv2, inc="S")
            if dn:
                evd = v(lambda e, b=bank, bu=buf, jj=j, nn=n, l=L:
                        e.tensor_tensor(bu[:, l - 128:l],
                                        pA[b][:, nn - 128:nn], cmask[:], A.add),
                        wait=wv2 + [evG] + ([ev_s] if n - dn > 0 else []),
                        inc="V")
            ev_scA[bank] = evd if dn else ev_s
        ev_sc_done[qt] = (ev_scA[0], ev_scA[1], evd)

    def emit_softmax_trans_vmm(qt):
        L = (qt + 1) * 128
        buf = scb[qt % 2]
        wv = list(ev_sc_done[qt])
        if qt >= 1:
            wv.append(ev_exp[qt - 1])
        e2 = vr(lambda e, l=L, bu=buf: e.reduce_max(mx_c[:, 0:1], bu[:, 0:l],
                                                    axis=AX.X),
                wait=wv, inc="V")
        ev_rowmax[qt] = e2
        e3 = v(lambda e: e.tensor_scalar(mx_c[:, 1:2], mx_c[:, 0:1], -1.0, None,
                                         A.mult), wait=[e2], inc="V")
        e4 = s(lambda e, l=L, bu=buf: e.activation(probs[:, 0:l], bu[:, 0:l],
                                                   AF.Exp, bias=mx_c[:, 1:2],
                                                   accum_out=mx_c[:, 2:3]),
               wait=[e3] + ([ev_norm[qt - 1]] if qt >= 1 else []), inc="S")
        ev_exp[qt] = e4
        s(lambda e: e.activation(mx_c[:, 3:4], mx_c[:, 2:3], AF.Ln), inc="S")
        e5 = s(lambda e: e.activation(mx_c[:, 4:5], mx_c[:, 3:4], AF.Exp,
                                      scale=-1.0), inc="S")
        ev_ptev = [None] * (qt + 1)
        for kt in range(qt + 1):
            bank = kt % 2
            evt = t_tr(pT[bank][:, 0:128], probs[:, kt * 128:(kt + 1) * 128],
                       identb[:], wait=[e4, ev_ptA[bank]], inc="T")
            eve = v(lambda e, k=kt, b=bank:
                    e.tensor_copy(probsT[:, k * 128:(k + 1) * 128],
                                  pT[b][:, 0:128]),
                    wait=[evt], inc="V")
            ev_ptA[bank] = eve
            ev_ptev[kt] = eve
        for dh in range(2):
            evm = None
            for kt in range(qt + 1):
                wv2 = [ev_ptev[kt]]
                if kt == 0 and ev_pB_free[dh]:
                    wv2.append(ev_pB_free[dh])
                evm = t_mm(pB[dh][:], probsT[:, kt * 128:(kt + 1) * 128],
                           harena[:, kt * D + dh * 512:kt * D + (dh + 1) * 512],
                           start=(kt == 0), stop=(kt == qt),
                           wait=wv2, inc="T" if kt == qt else None)
            ev1 = v(lambda e, d=dh:
                    e.scalar_tensor_tensor(scrA[:, 0:512], pB[d][:],
                                           mx_c[:, 4:5],
                                           ga_b[:, d * 512:(d + 1) * 512],
                                           A.mult, A.mult),
                    wait=[evm, ev_gab, e5], inc="V")
            ev_pB_free[dh] = ev1
            ev_att_res[qt] = v(lambda e, q=qt, d=dh:
                               e.tensor_add(x_v[:, q, d * 512:(d + 1) * 512],
                                            x_v[:, q, d * 512:(d + 1) * 512],
                                            scrA[:, 0:512]),
                               wait=[ev1], inc="V")
        ev_norm[qt] = ev_att_res[qt]

    naqt = int(os.environ.get("KAQT", str(NTS)))
    emit_scores(0)
    for qt in range(naqt):
        if qt + 1 < naqt:
            emit_scores(qt + 1)
        emit_softmax_trans_vmm(qt)
    if naqt < NTS:
        for qx in range(naqt, NTS):
            ev_att_res[qx] = ev_att_res[naqt - 1]
            ev_exp[qx] = ev_exp[naqt - 1]
            ev_rowmax[qx] = ev_rowmax[naqt - 1]

    if phases < 5:
        return _finish(nc, p, sems, ctx, dbg, locals())
    ev_stat2 = []
    for ts in range(NTS):
        wx = [ev_att_res[ts]]
        e1 = vr(lambda e, t=ts: e.reduce_sum(sum_c[:, t:t + 1], x_v[:, t, :],
                                             axis=AX.X), wait=wx, inc="V")
        e2 = s(lambda e, t=ts: e.activation(scrB, x_v[:, t, :], AF.Square,
                                            accum_out=ssq_c[:, t:t + 1]),
               wait=wx + [ev_exp[NTS - 1]], inc="S")
        ev_stat2.append((e1, e2))
    ev_ln2 = ln_cols(ev_stat2[-1][0], ev_stat2[-1][1])

    ev_h2T = [None] * NTS
    ev_slot = [None, None]
    ev_oslot = [None, None]
    ev_ptL = ev_ptA[:]
    h2slots = [probs[:, 0:1024], probs[:, 1024:2048]]
    for ts in range(NTS):
        slot = [scrA, scrB][ts % 2]
        oslot = h2slots[ts % 2]
        wv = [ev_ln2]
        if ev_slot[ts % 2]:
            wv.append(ev_slot[ts % 2])
        e1 = s(lambda e, t=ts, sl=slot:
               e.activation(sl, x_v[:, t, :], AF.Identity,
                            scale=rs_c[:, t:t + 1], bias=nm_c[:, t:t + 1]),
               wait=wv, inc="S")
        e2 = v(lambda e, sl=slot: e.scalar_tensor_tensor(scrC, cb_b[:], 1.0, sl,
                                                         A.add, A.mult),
               wait=[e1, ev_cb], inc="V")
        ev_slot[ts % 2] = e2
        e3 = v(lambda e, osl=oslot: e.tensor_add(osl, scrC, sb_b[:]),
               wait=[ev_sb2, e2] + ([ev_oslot[ts % 2]] if ev_oslot[ts % 2] else []),
               inc="V")
        last_tr = None
        eve = None
        for dc in range(NDC):
            bank = dc % 2
            last_tr = t_tr(pT[bank][:, 0:128], oslot[:, dc * 128:(dc + 1) * 128],
                           identb[:], wait=[e3, ev_ptL[bank]], inc="T")
            eve = v(lambda e, t=ts, c=dc, b=bank:
                    e.tensor_copy(harena[:, c * S + t * 128:c * S + (t + 1) * 128],
                                  pT[b][:, 0:128]),
                    wait=[last_tr], inc="V")
            ev_ptL[bank] = eve
        ev_oslot[ts % 2] = last_tr
        ev_h2T[ts] = eve
    ev_h2T_all = ev_h2T[-1]

    if phases < 6:
        return _finish(nc, p, sems, ctx, dbg, locals())
    ev_b2d = gd(b2rows, hb2[:, :],
                wait=[ev_att_res[NTS - 1], ev_exp[NTS - 1]], sem="SM")
    ev_b2gb = v(lambda e: e.tensor_tensor(b2gb, b2rows, gb_b[0:8, :], A.mult),
                wait=[ev_b2d, ev_gbb], inc="V")
    ev_sel = [None] * NTS
    ev_selT = [None] * NTS
    ev_gm_prev = [None, None]
    ev_selbf_used = None
    ev_ptG = ev_ptL[:]
    for ts in range(NTS):
        bank = ts % 2
        wv = [ev_h2T[ts], evSM, ev_pc_free[bank]]
        evm = None
        for dc in range(NDC):
            evm = t_mm(pC[bank][:, 0:E],
                       harena[:, dc * S + ts * 128:dc * S + (ts + 1) * 128],
                       wg_sb[:, dc * E:(dc + 1) * E],
                       start=(dc == 0), stop=(dc == NDC - 1),
                       wait=wv if dc == 0 else (), inc="T" if dc == NDC - 1 else None)
        ega = v(lambda e, t=ts, b=bank:
                e.tensor_tensor(g_tok[:, t * E:(t + 1) * E], pC[b][:, 0:E],
                                bg_bc[:], A.add),
                wait=[evm, ev_bgbc], inc="V")
        ev_pc_free[bank] = ega
        v_nop()
        gm = tm_c[:, (ts % 2):(ts % 2) + 1]
        em = vr(lambda e, t=ts, g=gm: e.reduce_max(g, g_tok[:, t * E:(t + 1) * E],
                                                   axis=AX.X),
                wait=[ega] + ([ev_gm_prev[ts % 2]] if ev_gm_prev[ts % 2] else []),
                inc="V")
        ev_gm_prev[ts % 2] = em
        ev_sel[ts] = v(lambda e, t=ts, g=gm:
                       e.tensor_scalar(sel[:, t * E:(t + 1) * E],
                                       g_tok[:, t * E:(t + 1) * E], g, None,
                                       A.is_equal),
                       wait=[em], inc="V")
        v_nop()
        evbf = v(lambda e, t=ts: e.tensor_copy(selbf, sel[:, t * E:(t + 1) * E]),
                 wait=[ev_b2d, ev_selbf_used, ev_sel[ts]], inc="V")
        evt2 = t_tr(pT[ts % 2][0:8, 0:128], selbf, identb[:],
                    wait=[evbf, ev_ptG[ts % 2]], inc="T")
        ev_selbf_used = evt2
        ev_selT[ts] = v(lambda e, t=ts, b=ts % 2:
                        e.tensor_copy(selT[0:8, t * 128:(t + 1) * 128],
                                      pT[b][0:8, 0:128]),
                        wait=[evt2], inc="V")
        ev_ptG[ts % 2] = ev_selT[ts]

    if phases < 7:
        return _finish(nc, p, sems, ctx, dbg, locals())
    w1_r = [hw1[e].rearrange("(c q) h -> q c h", q=128) for e in range(E)]
    w2_r = [hw2[e].rearrange("(a c q) d -> q a c d", a=NQ, c=HCQ) for e in range(E)]
    w1b_full = [w1b[i][:].rearrange("q (c m) -> q c m", c=NDC) for i in range(2)]
    w2b_full = [w2b[i][:].rearrange("q (c m) -> q c m", c=HCQ) for i in range(2)]

    ev_wq_cons = [ev_modmm[10], ev_modmm[11]]
    ev_pA_free = ev_scA[:]
    ev_pB_free2 = ev_pB_free[:]
    ev_outsb_S = [None] * NTS
    ev_gating = [None] * NTS
    ev_mm2_done_prev = None

    nfe = int(os.environ.get("KFE", str(E)))
    for g in range(nfe * NQ):
        e_idx, Q = divmod(g, NQ)
        par = g % 2
        wv = [ev_wq_cons[par]] if ev_wq_cons[par] else []
        evw1 = gd(w1b_full[par], w1_r[e_idx][:, :, Q * D:(Q + 1) * D],
                  wait=wv, sem="W")
        evw2 = gd(w2b_full[par], w2_r[e_idx][:, Q], sem="W")
        ev_mm2_t_done = None
        for tc in range(4):
            ev_h1T = [None] * HCQ
            for hcl in range(HCQ):
                bank = (tc * HCQ + hcl) % 2
                wv = [evw1, ev_h2T[tc * 4 + 3]]
                if ev_pA_free[bank]:
                    wv.append(ev_pA_free[bank])
                evm = None
                for dc in range(NDC):
                    evm = t_mm(pA[bank][:],
                               w1b[par][:, dc * D + hcl * 128:dc * D + (hcl + 1) * 128],
                               harena[:, dc * S + tc * 512:dc * S + (tc + 1) * 512],
                               start=(dc == 0), stop=(dc == NDC - 1),
                               wait=wv if dc == 0 else (),
                               inc="T" if dc == NDC - 1 else None)
                hc = Q * HCQ + hcl
                wv2 = [evm, ev_b1_done]
                if tc > 0:
                    wv2.append(ev_mm2_t_done)
                elif ev_mm2_done_prev:
                    wv2.append(ev_mm2_done_prev)
                ev_h1T[hcl] = s(lambda e, hl=hcl, h=hc, ei=e_idx, b=bank:
                                e.activation(h1T[:, hl * 512:(hl + 1) * 512],
                                             pA[b][:], AF.Relu,
                                             bias=b1c[:, h * E + ei:h * E + ei + 1]),
                                wait=wv2, inc="S")
                ev_pA_free[bank] = ev_h1T[hcl]
            for ts_l in range(4):
                ts_g = tc * 4 + ts_l
                for dh in range(2):
                    bank = (ts_l * 2 + dh) % 2
                    wv = [evw2]
                    if ev_pB_free2[bank]:
                        wv.append(ev_pB_free2[bank])
                    evm = None
                    for hcl in range(HCQ):
                        evm = t_mm(pB[bank][:],
                                   h1T[:, hcl * 512 + ts_l * 128:hcl * 512 + (ts_l + 1) * 128],
                                   w2b[par][:, hcl * D + dh * 512:hcl * D + (dh + 1) * 512],
                                   start=(hcl == 0), stop=(hcl == HCQ - 1),
                                   wait=(wv + [ev_h1T[hcl]]) if hcl == 0 else [ev_h1T[hcl]],
                                   inc="T" if hcl == HCQ - 1 else None)
                    ev_mm2_t_done = evm
                    osl = out_sb[:, ts_g * D + dh * 512:ts_g * D + (dh + 1) * 512]
                    if Q == 0:
                        wv3 = [evm]
                        if e_idx > 0 and ev_gating[ts_g]:
                            wv3.append(ev_gating[ts_g])
                        eve = s(lambda e, o=osl, b=bank:
                                e.activation(o, pB[b][:], AF.Identity),
                                wait=wv3, inc="S")
                        ev_outsb_S[ts_g] = eve
                    else:
                        eve = v(lambda e, o=osl, b=bank:
                                e.tensor_add(o, o, pB[b][:]),
                                wait=[evm, ev_outsb_S[ts_g]], inc="V")
                    ev_pB_free2[bank] = eve
        ev_wq_cons[par] = ev_mm2_t_done
        ev_mm2_done_prev = ev_mm2_t_done
        if Q == NQ - 1:
            for ts in range(NTS):
                evst = v(lambda e, t=ts, ei=e_idx:
                         e.scalar_tensor_tensor(scrA, out_sb[:, t * D:(t + 1) * D],
                                                sel[:, t * E + ei:t * E + ei + 1],
                                                gb_b[:], A.mult, A.mult),
                         wait=[ev_sel[ts], ev_gbb, ev_pB_free2[0],
                               ev_pB_free2[1]], inc="V")
                ev_gating[ts] = v(lambda e, t=ts:
                                  e.tensor_add(x_v[:, t, :], x_v[:, t, :], scrA),
                                  wait=[evst], inc="V")

    if nfe < E:
        for ts in range(NTS):
            if ev_gating[ts] is None:
                ev_gating[ts] = ("V", 0)
    ev_final = [None] * NTS
    for ts in range(NTS):
        for dh in range(2):
            bank = (ts * 2 + dh) % 2
            wv = [ev_selT[ts], ev_b2gb, ev_gating[ts], ev_pc_free[bank]]
            evm = t_mm(pC[bank][:], selT[0:8, ts * 128:(ts + 1) * 128],
                       b2gb[0:8, dh * 512:(dh + 1) * 512],
                       start=True, stop=True, wait=wv, inc="T")
            ev_final[ts] = v(lambda e, t=ts, d=dh, b=bank:
                             e.tensor_add(x_v[:, t, d * 512:(d + 1) * 512],
                                          x_v[:, t, d * 512:(d + 1) * 512],
                                          pC[b][:]),
                             wait=[evm], inc="V")
            ev_pc_free[bank] = ev_final[ts]

    return _finish(nc, p, sems, ctx, dbg, locals())


def _finish(nc, p, sems, ctx, dbg, L):
    gd = L["gd"]
    x_v, hout = L["x_v"], L["hout"]
    n_out = 0
    if L.get("ev_final") and L["ev_final"][0] is not None:
        out_r = hout.rearrange("(ts q) d -> q ts d", q=128)
        for i in range(4):
            wv = [L["ev_final"][i * 4 + k] for k in range(4)]
            gd(out_r[:, i * 4:(i + 1) * 4, :], x_v[:, i * 4:(i + 1) * 4, :],
               wait=wv, sem="O")
        n_out = 4
    dbg_srcs = {
        "mod": L["mod_row"], "hv": L["harena"][:, 0:2048],
        "xattn": L["x_sb"][:, 0:2048], "qT": L["qT"][0:QK, 0:512],
        "kT": L["kT"][0:QK, 0:512], "probs0": L["probs"][:, 0:1024],
        "h2T": L["harena"][:, 0:2048], "gtok": L["g_tok"][:],
        "sel": L["sel"][:], "outsb": L["out_sb"][:, 0:2048],
        "cab": L["ca_b"][:], "sab": L["sa_b"][:], "gab": L["ga_b"][:],
        "gbb": L["gb_b"][:], "b1c": L["b1c"][:, 0:256],
        "h1T": L["h1T"][:, 0:2048], "x0": L["x_sb"][:, 0:2048],
        "scores": L["scb"][1][:, 0:2048], "mxc": L["mx_c"][:], "sumc": L["sum_c"][:],
        "idxf": L["idxf"][:], "max8": L["max8"][:], "xall": L["x_sb"][:], "selTd": L["selT"], "b2gbd": L["b2gb"],
        "ssqc": L["ssq_c"][:], "mc": L["m_c"][:], "rsc": L["rs_c"][:],
        "nmc": L["nm_c"][:],
    }
    for nm in dbg:
        src_ap = dbg_srcs[nm]
        hd = nc.dram_tensor("dbg_" + nm, list(src_ap.shape), F32,
                            kind="ExternalOutput").ap()
        gd(hd[:, :], src_ap,
           wait=[("V", p.ctr.get("V", 0)), ("S", p.ctr.get("S", 0)),
                 ("T", p.ctr.get("T", 0)), ("X", p.ctr.get("X", 0))], sem="O")
        n_out += 1
    p.on("gpsimd", lambda g: None, wait=[("O", 16 * n_out)])
    p.on("sync", lambda sy: None, wait=[("O", 16 * n_out)])

    block = nc.Block()
    with block:
        @block.tensor
        def _(t):
            p.emit("tensor", t, sems)

        @block.vector
        def _(e):
            p.emit("vector", e, sems)

        @block.scalar
        def _(e):
            p.emit("scalar", e, sems)

        @block.sync
        def _(e):
            p.emit("sync", e, sems)

        @block.gpsimd
        def _(e):
            p.emit("gpsimd", e, sems)
    return nc


_NC_CACHE = {}


def _get_nc(dbg=(), phases=99):
    key = (tuple(dbg), phases)
    if key not in _NC_CACHE:
        _NC_CACHE[key] = build_nc(tuple(dbg), phases)
    return _NC_CACHE[key]


def kernel(L0_attn_x, L0_attn_w_q, L0_attn_b_q, L0_attn_w_k, L0_attn_b_k,
           L0_ffn_Wg, L0_ffn_bg, L0_ffn_W1, L0_ffn_b1, L0_ffn_W2, L0_ffn_b2,
           L0_c, L0_wmod, _dbg=(), _phases=99):
    nc = _get_nc(_dbg, _phases)
    f32 = np.float32
    asc = np.ascontiguousarray

    def arr(a):
        return asc(np.asarray(a, dtype=f32))

    shared = {
        "w_q": arr(L0_attn_w_q), "b_q": arr(L0_attn_b_q).reshape(QK, 1),
        "w_k": arr(L0_attn_w_k), "b_k": arr(L0_attn_b_k).reshape(QK, 1),
        "wg": arr(L0_ffn_Wg), "bg_r": arr(L0_ffn_bg).reshape(1, E),
        "w1": arr(L0_ffn_W1), "b1": arr(L0_ffn_b1),
        "w2": arr(L0_ffn_W2), "b2": arr(L0_ffn_b2),
        "wmod": arr(L0_wmod),
    }
    x = np.asarray(L0_attn_x, dtype=f32)
    c = np.asarray(L0_c, dtype=f32)
    in_maps = []
    for i in range(8):
        m = dict(shared)
        m["x_in"] = asc(x[i])
        m["cvec"] = asc(c[i:i + 1])
        in_maps.append(m)
    res = run_bass_kernel_spmd(nc, in_maps, core_ids=list(range(8)))
    if _phases < 99:
        return None, res
    out = np.stack([r["out"] for r in res.results], axis=0)
    if _dbg:
        return out, res
    return out
